# version 02
# baseline (speedup 1.0000x reference)
import math
from contextlib import ExitStack

import numpy as np
import ml_dtypes

import concourse.bass as bass
import concourse.mybir as mybir
from concourse.bass_utils import run_bass_kernel_spmd

F32 = mybir.dt.float32
BF16 = mybir.dt.bfloat16
I32 = mybir.dt.int32
AF = mybir.ActivationFunctionType
ALU = mybir.AluOpType
AX = mybir.AxisListType
BF = ml_dtypes.bfloat16

NEG_BIG = -1.0e30


def default_cfg():
    return dict(
        S=4096,
        D=1024,
        H=16,
        HD=64,
        LAT=512,
        F0=2730,
        NG=4,
        E0=8,
        E1=8,
        HID1=4096,
        R=8,
        TB=512,
        FTB=512,
        NTOK=8 * 4096,
        EPS=1e-5,
        ROPE_THETA=10000.0,
    )



class Prog:

    def __init__(self, name="prog"):
        self.nc = bass.Bass("TRN2", target_bir_lowering=False)
        self.stack = ExitStack()
        self.sem_vals = {}
        self.sems = {}
        self.tensors = {}

    def sem(self, name):
        if name not in self.sems:
            h = self.stack.enter_context(self.nc.semaphore(name))
            self.sems[name] = h
            self.sem_vals[name] = 0
        return self.sems[name]

    def inc(self, inst, name, n=1):
        inst.then_inc(self.sem(name), n)
        self.sem_vals[name] += n
        return self.sem_vals[name]

    def val(self, name):
        self.sem(name)
        return self.sem_vals[name]

    def wait(self, eng, name, val=None):
        eng.wait_ge(self.sem(name), self.sem_vals[name] if val is None else val)

    def sbuf(self, name, shape, dtype):
        h = self.stack.enter_context(self.nc.sbuf_tensor(name, shape, dtype))
        self.tensors[name] = h
        return h

    def psum(self, name, shape, dtype=F32):
        h = self.stack.enter_context(self.nc.psum_tensor(name, shape, dtype))
        self.tensors[name] = h
        return h

    def din(self, name, shape, dtype):
        ap = self.nc.dram_tensor(name, shape, dtype, kind="ExternalInput").ap()
        self.tensors[name] = ap
        return ap

    def dout(self, name, shape, dtype):
        ap = self.nc.dram_tensor(name, shape, dtype, kind="ExternalOutput").ap()
        self.tensors[name] = ap
        return ap

    def dtmp(self, name, shape, dtype):
        ap = self.nc.dram_tensor(name, shape, dtype, kind="Internal").ap()
        self.tensors[name] = ap
        return ap

    def dma(self, eng, out, in_, sem_name, **kw):
        inst = eng.dma_start(out, in_, **kw)
        return self.inc(inst, sem_name, 16)

    def eng(self, name):
        if not hasattr(self, "_engs"):
            self._engs = {}
        if name not in self._engs:
            raw = {"pe": self.nc.tensor, "dve": self.nc.vector,
                   "act": self.nc.scalar, "gp": self.nc.gpsimd,
                   "sp": self.nc.sync}[name]
            self._engs[name] = Eng(self, raw, name)
        return self._engs[name]

    def chains(self):
        return (self.eng("pe"), self.eng("dve"), self.eng("act"),
                self.eng("gp"), self.eng("sp"))

    def finish(self):
        self.stack.close()
        return self.nc


class Eng:

    def __init__(self, p: Prog, raw, name):
        self.p = p
        self.raw = raw
        self.name = name
        self.semname = name + "_tick"
        self.t = 0
        self._pending = []

    def after(self, other, tick=None):
        if isinstance(other, Eng):
            self._pending.append((other.semname, other.t if tick is None else tick))
        else:
            self._pending.append((other, tick))
        return self

    def op(self, fname, *a, **kw):
        for semname, v in self._pending:
            self.raw.wait_ge(self.p.sem(semname), v)
        self._pending.clear()
        if self.t:
            self.raw.wait_ge(self.p.sem(self.semname), self.t)
        inst = getattr(self.raw, fname)(*a, **kw)
        self.t = self.p.inc(inst, self.semname)
        return inst

    def dma(self, out, in_, sem_name, **kw):
        for semname, v in self._pending:
            self.raw.wait_ge(self.p.sem(semname), v)
        self._pending.clear()
        inst = self.raw.dma_start(out, in_, **kw)
        return self.p.inc(inst, sem_name, 16)

    def wait_dma(self, sem_name, val=None):
        self._pending.append((sem_name, self.p.val(sem_name) if val is None else val))
        return self


def declare_consts(p: Prog):
    p.din("c_ones_col", [128, 1], F32)
    p.din("c_ones_row", [1, 128], F32)
    p.din("c_one1", [1, 1], F32)
    p.din("c_ones_row2", [2, 128], F32)
    p.din("c_ones_col_bf", [128, 1], BF16)


def const_arrays():
    return {
        "c_ones_col": np.ones((128, 1), np.float32),
        "c_ones_row": np.ones((1, 128), np.float32),
        "c_one1": np.ones((1, 1), np.float32),
        "c_ones_row2": np.ones((2, 128), np.float32),
        "c_ones_col_bf": np.ones((128, 1), BF),
    }



def emit_ln_router(p: Prog, cfg, layer: int):
    nc = p.nc
    pe, dve, act, gp, sp = p.chains()
    S, D, TB = cfg["S"], cfg["D"], cfg["TB"]
    E = cfg["E0"] if layer == 0 else cfg["E1"]
    NC = D // 128
    NTB = S // TB
    NTT = TB // 128

    x = p.tensors["x_fm"]
    wr = p.tensors["w_r"]
    br = p.tensors["b_r"]
    xn = p.tensors["xn_fm"]
    rout = p.tensors["probs" if layer == 0 else "combine"]

    ones_col = p.tensors["c_ones_col"]
    ones_row = p.tensors["c_ones_row"]
    one1 = p.tensors["c_one1"]

    xb = p.sbuf("lr_x", [128, NC * TB], F32)
    xsq = p.sbuf("lr_xsq", [128, NC * TB], F32)
    wrb = p.sbuf("lr_wr", [128, NC * E], F32)
    csbr = p.sbuf("lr_csbr", [2, E], F32)
    brow2 = p.sbuf("lr_brow2", [2, 128], F32)
    onc = p.sbuf("lr_onc", [128, 1], F32)
    onr = p.sbuf("lr_onr", [1, 128], F32)
    on1 = p.sbuf("lr_on1", [1, 1], F32)
    rows = p.sbuf("lr_rows", [1, 8 * TB], F32)
    ab_sb = p.sbuf("lr_absb", [128, 2 * TB], F32)
    xnb = p.sbuf("lr_xn", [128, NC * TB], BF16)
    tmp = p.sbuf("lr_tmp", [128, NC * TB], F32)
    acol = p.sbuf("lr_acol", [128, 1], F32)
    bc8 = p.sbuf("lr_bc8", [128, E], F32)
    lg = p.sbuf("lr_lg", [128, 6 * E + 8], F32)

    ps_sx = p.psum("lr_ps_sx", [1, TB])
    ps_sq = p.psum("lr_ps_sq", [1, TB])
    ps_ab = p.psum("lr_ps_ab", [128, 2 * TB])
    ps_lg = p.psum("lr_ps_lg", [128, E])
    ps_c1 = p.psum("lr_ps_c1", [128, 1])
    ps_c8 = p.psum("lr_ps_c8", [128, E])
    ps_cs = p.psum("lr_ps_cs", [1, E])

    sp.dma(onc[:], ones_col, "ld")
    sp.dma(onr[:], ones_row, "ld")
    sp.dma(on1[:], one1, "ld")
    sp.dma(brow2[1:2, :], ones_row, "ld")
    sp.dma(csbr[1:2, :], br, "ld")
    for c in range(NC):
        sp.dma(wrb[:, c * E:(c + 1) * E], wr[c * 128:(c + 1) * 128, :], "ld")
    pe.wait_dma("ld")
    for c in range(NC):
        pe.op("matmul", ps_cs[:], onc[:], wrb[:, c * E:(c + 1) * E],
              start=(c == 0), stop=(c == NC - 1))
    dve.after(pe).op("tensor_copy", csbr[0:1, :], ps_cs[:])

    for tb in range(NTB):
        t0 = tb * TB
        sp.after(pe).after(act).after(gp).after(dve)
        for c in range(NC):
            sp.dma(xb[:, c * TB:(c + 1) * TB],
                   x[c * 128:(c + 1) * 128, t0:t0 + TB], "xld")

        act.wait_dma("xld")
        for c in range(NC):
            act.op("activation", xsq[:, c * TB:(c + 1) * TB],
                   xb[:, c * TB:(c + 1) * TB], AF.Square)

        pe.after(act).after(dve)
        for c in range(NC):
            pe.op("matmul", ps_sx[:], onc[:], xb[:, c * TB:(c + 1) * TB],
                  start=(c == 0), stop=(c == NC - 1))
        for c in range(NC):
            pe.op("matmul", ps_sq[:], onc[:], xsq[:, c * TB:(c + 1) * TB],
                  start=(c == 0), stop=(c == NC - 1))

        m_r = rows[:, 0 * TB:0 * TB + TB]
        v_r = rows[:, 1 * TB:1 * TB + TB]
        a_r = rows[:, 2 * TB:2 * TB + TB]
        b_r = rows[:, 3 * TB:3 * TB + TB]
        t_r = rows[:, 4 * TB:4 * TB + TB]
        dve.after(pe)
        dve.op("tensor_scalar_mul", m_r, ps_sx[:], 1.0 / D)
        dve.op("tensor_scalar_mul", v_r, ps_sq[:], 1.0 / D)
        dve.op("tensor_mul", t_r, m_r, m_r)
        dve.op("tensor_sub", v_r, v_r, t_r)
        dve.op("tensor_scalar_add", v_r, v_r, cfg["EPS"])
        dve.op("reciprocal", t_r, v_r)
        act.after(dve).op("activation", a_r, t_r, AF.Sqrt)
        dve.after(act)
        dve.op("tensor_mul", t_r, m_r, a_r)
        dve.op("tensor_scalar_mul", b_r, t_r, -1.0)

        pe.after(dve)
        pe.op("matmul", ps_ab[:, 0:TB], onr[:], a_r, start=True, stop=True)
        pe.op("matmul", ps_ab[:, TB:2 * TB], onr[:], b_r, start=True, stop=True)
        dve.after(pe).after(gp).op("tensor_copy", ab_sb[:], ps_ab[:])

        gp.after(dve).wait_dma("st")
        for c in range(NC):
            gp.op("tensor_mul", tmp[:, c * TB:(c + 1) * TB],
                  xb[:, c * TB:(c + 1) * TB], ab_sb[:, 0:TB])
            gp.op("tensor_add", xnb[:, c * TB:(c + 1) * TB],
                  tmp[:, c * TB:(c + 1) * TB], ab_sb[:, TB:2 * TB])
        sp.after(gp)
        for c in range(NC):
            sp.dma(xn[c * 128:(c + 1) * 128, t0:t0 + TB],
                   xnb[:, c * TB:(c + 1) * TB], "st")

        for tt in range(NTT):
            q0 = tt * 128
            pe.after(dve)
            for c in range(NC):
                pe.op("matmul", ps_lg[:], xb[:, c * TB + q0:c * TB + q0 + 128],
                      wrb[:, c * E:(c + 1) * E],
                      start=(c == 0), stop=(c == NC - 1))
            pe.op("matmul", ps_c1[:], a_r[:, q0:q0 + 128], on1[:],
                  start=True, stop=True)
            dve.after(pe)
            dve.op("tensor_copy", acol[:], ps_c1[:])
            dve.op("tensor_copy", brow2[0:1, :], b_r[:, q0:q0 + 128])
            pe.after(dve).op("matmul", ps_c8[:], brow2[:], csbr[:],
                             start=True, stop=True)
            dve.after(pe)
            dve.op("tensor_copy", bc8[:], ps_c8[:])
            lgt = lg[:, 0:E]
            dve.op("tensor_single_scalar", lgt, ps_lg[:], acol[:], ALU.mult)
            dve.op("tensor_add", lgt, lgt, bc8[:])

            if layer == 0:
                mx = lg[:, E:E + 1]
                sm = lg[:, E + 1:E + 2]
                pr = lg[:, 2 * E:3 * E]
                rs = lg[:, E + 2:E + 3]
                dve.op("reduce_max", mx, lgt, axis=AX.X, negate=True)
                act.after(dve).wait_dma("st")
                act.op("activation", pr, lgt, AF.Exp, bias=mx, accum_out=sm)
                dve.after(act)
                dve.op("reciprocal", rs, sm)
                dve.op("tensor_single_scalar", pr, pr, rs, ALU.mult)
                sp.after(dve)
                sp.dma(rout[t0 + q0:t0 + q0 + 128, :], pr, "st")
            else:
                m1 = lg[:, E:E + 1]
                m2 = lg[:, E + 1:E + 2]
                g1w = lg[:, E + 2:E + 3]
                g2w = lg[:, E + 3:E + 4]
                dd = lg[:, E + 4:E + 5]
                eq1 = lg[:, 2 * E:3 * E]
                l2 = lg[:, 3 * E:4 * E]
                eq2 = lg[:, 4 * E:5 * E]
                cmb = lg[:, 5 * E:6 * E]
                dve.wait_dma("st")
                dve.op("reduce_max", m1, lgt, axis=AX.X)
                dve.op("tensor_single_scalar", eq1, lgt, m1, ALU.is_equal)
                dve.op("scalar_tensor_tensor", l2, eq1, NEG_BIG, lgt,
                       ALU.mult, ALU.add)
                dve.op("reduce_max", m2, l2, axis=AX.X)
                dve.op("tensor_single_scalar", eq2, l2, m2, ALU.is_equal)
                dve.op("tensor_sub", dd, m1, m2)
                act.after(dve).op("activation", g1w, dd, AF.Sigmoid)
                dve.after(act)
                dve.op("tensor_scalar", g2w, g1w, -1.0, 1.0, ALU.mult, ALU.add)
                dve.op("tensor_single_scalar", eq1, eq1, g1w, ALU.mult)
                dve.op("tensor_single_scalar", eq2, eq2, g2w, ALU.mult)
                dve.op("tensor_add", cmb, eq1, eq2)
                sp.after(dve)
                sp.dma(rout[t0 + q0:t0 + q0 + 128, :], cmb, "st")

    nc.sync.wait_ge(p.sem("st"), p.val("st"))



def alloc_proj_bufs(p: Prog, cfg, max_nci, max_ncio, rope, norms):
    TB = cfg["TB"]
    bufs = {
        "wf": p.sbuf("pj_wf", [128, max_nci * 128], F32),
        "wb": p.sbuf("pj_wb", [128, max_ncio * 128], BF16),
        "xbuf": p.sbuf("pj_x", [128, max_nci * TB], BF16),
        "ev": p.sbuf("pj_ev", [128, TB], BF16),
        "ps": p.psum("pj_ps", [128, TB]),
    }
    if rope:
        bufs["w2b"] = p.sbuf("pj_w2b", [128, max_ncio * 128], BF16)
        bufs["ev2"] = p.sbuf("pj_ev2", [128, TB], BF16)
        bufs["evr"] = p.sbuf("pj_evr", [128, 2 * TB], BF16)
        bufs["ps2"] = p.psum("pj_ps2", [128, TB])
    if norms:
        bufs["sq"] = p.sbuf("pj_sq", [128, TB], F32)
        bufs["psn"] = p.psum("pj_psn", [1, TB])
    return bufs


def emit_projection(p: Prog, cfg, tag, bufs, src_name, w_name, dst_name,
                    Din, Dout, mode="fm", rope=False, norms=None, w2_name=None):
    nc = p.nc
    pe, dve, act, gp, sp = p.chains()
    S, TB = cfg["S"], cfg["TB"]
    NCI = Din // 128
    NCO = (Dout + 127) // 128
    NTB = S // TB
    src = p.tensors[src_name]
    dst = p.tensors[dst_name]
    w = p.tensors[w_name]

    wf, wb, xbuf, ev, ps = (bufs["wf"], bufs["wb"], bufs["xbuf"], bufs["ev"],
                            bufs["ps"])
    w2b = bufs.get("w2b")
    ev2 = bufs.get("ev2")
    evr = bufs.get("evr")
    sq = bufs.get("sq")
    ps2 = bufs.get("ps2")
    psn = bufs.get("psn")

    wsrcs = ((w, wb),) + (((p.tensors[w2_name], w2b),) if rope else ())
    for ws, wdst in wsrcs:
        for oc in range(NCO):
            o0 = oc * 128
            ow = min(128, Dout - o0)
            for c in range(NCI):
                sp.after(gp)
                sp.dma(wf[:, c * 128:c * 128 + ow],
                       ws[c * 128:(c + 1) * 128, o0:o0 + ow], "pwld")
            gp.wait_dma("pwld").after(pe)
            for c in range(NCI):
                gp.op("tensor_copy",
                      wdst[:, (oc * NCI + c) * 128:(oc * NCI + c) * 128 + ow],
                      wf[:, c * 128:c * 128 + ow])
            if ws is not w:
                dve.after(gp)
                for c in range(NCI):
                    base = (oc * NCI + c) * 128
                    for neg0 in (0, 64):
                        nw = max(0, min(32, ow - neg0))
                        if nw:
                            dve.op("tensor_scalar_mul",
                                   wdst[:, base + neg0:base + neg0 + nw],
                                   wdst[:, base + neg0:base + neg0 + nw], -1.0)
                gp.after(dve)
                gp.op("engine_nop")

    for tb in range(NTB):
        t0 = tb * TB
        sp.after(pe).after(gp).after(act)
        for c in range(NCI):
            sp.dma(xbuf[:, c * TB:(c + 1) * TB],
                   src[c * 128:(c + 1) * 128, t0:t0 + TB], "pxld")
        act.wait_dma("pst")
        gp.wait_dma("pst")
        pe.wait_dma("pxld").after(gp)
        for oc in range(NCO):
            o0 = oc * 128
            ow = min(128, Dout - o0)
            if mode == "fm":
                pe.after(act).after(dve).after(gp)
                for c in range(NCI):
                    pe.op("matmul", ps[0:ow, :],
                          wb[:, (oc * NCI + c) * 128:(oc * NCI + c) * 128 + ow],
                          xbuf[:, c * TB:(c + 1) * TB],
                          start=(c == 0), stop=(c == NCI - 1))
                if rope:
                    for c in range(NCI):
                        pe.op("matmul", ps2[0:ow, :],
                              w2b[:, (oc * NCI + c) * 128:(oc * NCI + c) * 128 + ow],
                              xbuf[:, c * TB:(c + 1) * TB],
                              start=(c == 0), stop=(c == NCI - 1))
                act.after(pe).wait_dma("pst")
                act.op("activation", ev[0:ow, :], ps[0:ow, :], AF.Copy)
                if norms:
                    act.op("activation", sq[0:ow, :], ps[0:ow, :], AF.Square)
                    H_per = ow // cfg["HD"]
                    for hh in range(H_per):
                        h = (o0 // cfg["HD"]) + hh
                        pb = hh * cfg["HD"]
                        pe.after(act).after(dve)
                        pe.op("matmul", psn[:],
                              p.tensors["sb_onc"][pb:pb + cfg["HD"], 0:1],
                              sq[pb:pb + cfg["HD"], :], start=True, stop=True)
                        dve.after(pe)
                        nrow = norms["row"]
                        dve.op("reduce_max",
                               nrow[0:1, h * NTB + tb:h * NTB + tb + 1],
                               psn[:], axis=AX.X)
                if rope:
                    act.op("activation", ev2[0:ow, :], ps2[0:ow, :], AF.Copy)
                    cosb = p.tensors["sb_cos2"]
                    sinb = p.tensors["sb_sin2"]
                    gp.after(act)
                    gp.op("tensor_mul", evr[:, 0:TB][0:ow, :], ev[0:ow, :],
                          cosb[0:ow, t0:t0 + TB])
                    gp.op("tensor_mul", evr[:, TB:2 * TB][0:ow, :], ev2[0:ow, :],
                          sinb[0:ow, t0:t0 + TB])
                    gp.op("tensor_add", ev[0:ow, :], evr[:, 0:TB][0:ow, :],
                          evr[:, TB:2 * TB][0:ow, :])
                    sp.after(gp)
                else:
                    sp.after(act)
                sp.dma(dst[o0:o0 + ow, t0:t0 + TB], ev[0:ow, :], "pst")
            else:
                for ttile in range(TB // 128):
                    pe.after(act)
                    for c in range(NCI):
                        pe.op("matmul", ps[:, 0:ow],
                              xbuf[:, c * TB + ttile * 128:c * TB + ttile * 128 + 128],
                              wb[:, (oc * NCI + c) * 128:(oc * NCI + c) * 128 + ow],
                              start=(c == 0), stop=(c == NCI - 1))
                    act.after(pe).wait_dma("pst")
                    act.op("activation", ev[:, 0:ow], ps[:, 0:ow], AF.Copy)
                    sp.after(act)
                    sp.dma(dst[t0 + ttile * 128:t0 + ttile * 128 + 128, o0:o0 + ow],
                           ev[:, 0:ow], "pst")
    return p.val("pst")


def emit_attn(p: Prog, cfg, layer: int):
    nc = p.nc
    pe, dve, act, gp, sp = p.chains()
    S, D, H, HD, TB = cfg["S"], cfg["D"], cfg["H"], cfg["HD"], cfg["TB"]
    NC = D // 128
    NTB = S // TB
    NKT = S // 128
    QD = H * HD
    scale0 = 1.0 / math.sqrt(D)
    scale1 = 1.0 / math.sqrt(HD)

    xn = p.tensors["xn_fm"]
    a_out = p.tensors["attn_fm"]

    onc = p.sbuf("sb_onc", [128, 1], F32)
    oncb = p.sbuf("sb_oncb", [128, 1], BF16)
    onr = p.sbuf("sb_onr", [1, 128], F32)
    sp.dma(onc[:], p.tensors["c_ones_col"], "ld")
    sp.dma(oncb[:], p.tensors["c_ones_col_bf"], "ld")
    sp.dma(onr[:], p.tensors["c_ones_row"], "ld")

    nrow_q = p.sbuf("at_nq", [1, H * NTB], F32)
    nrow_k = p.sbuf("at_nk", [1, H * NTB], F32)
    negc = p.sbuf("at_negc", [1, H * NTB + 2 * H], F32)

    if layer == 1:
        cos2 = p.sbuf("sb_cos2", [128, S], F32)
        sin2 = p.sbuf("sb_sin2", [128, S], F32)
        sp.dma(cos2[:], p.tensors["c_cos2"], "ld")
        sp.dma(sin2[:], p.tensors["c_sin2"], "ld")
        gp.wait_dma("ld")
    else:
        gb = p.sbuf("at_gb", [128, 2 * (D // 128)], F32)
        for c in range(D // 128):
            sp.dma(gb[:, c:c + 1],
                   p.tensors["ln_gamma"][c * 128:(c + 1) * 128, :], "ld")
            sp.dma(gb[:, D // 128 + c:D // 128 + c + 1],
                   p.tensors["ln_beta"][c * 128:(c + 1) * 128, :], "ld")

    pe.wait_dma("ld")
    dve.wait_dma("ld")
    act.wait_dma("ld")
    gp.wait_dma("ld")

    q_fm = p.dtmp("at_q", [QD, S], BF16)
    k_fm = p.dtmp("at_k", [QD, S], BF16)
    v_tm = p.dtmp("at_v", [S, QD], BF16)
    if layer == 0:
        bufs = alloc_proj_bufs(p, cfg, D // 128, (D // 128) * (QD // 128),
                               rope=False, norms=True)
        st1 = emit_projection(p, cfg, "pq", bufs, "xn_fm", "w_q", "at_q", D, QD,
                              "fm", norms={"row": nrow_q})
        st2 = emit_projection(p, cfg, "pk", bufs, "xn_fm", "w_k", "at_k", D, QD,
                              "fm", norms={"row": nrow_k})
        st3 = emit_projection(p, cfg, "pv", bufs, "xn_fm", "w_v", "at_v", D, QD,
                              "tm")
    else:
        LAT = cfg["LAT"]
        bufs = alloc_proj_bufs(p, cfg, D // 128, (D // 128) * (QD // 128),
                               rope=True, norms=True)
        kv_fm = p.dtmp("at_kv", [LAT, S], BF16)
        stkv = emit_projection(p, cfg, "pkv", bufs, "xn_fm", "w_kvd", "at_kv",
                               D, LAT, "fm")
        st1 = emit_projection(p, cfg, "pq", bufs, "xn_fm", "w_q", "at_q", D, QD,
                              "fm", rope=True, norms={"row": nrow_q},
                              w2_name="w_q2")
        pe.wait_dma("pst", stkv)
        sp.wait_dma("pst", stkv)
        st2 = emit_projection(p, cfg, "pk", bufs, "at_kv", "w_ku", "at_k", LAT,
                              QD, "fm", rope=True, norms={"row": nrow_k},
                              w2_name="w_ku2")
        st3 = emit_projection(p, cfg, "pv", bufs, "at_kv", "w_vu", "at_v", LAT,
                              QD, "tm")

    scale = scale0 if layer == 0 else scale1
    nkm = negc[0:1, H * NTB:H * NTB + H]
    dve.op("reduce_max", nkm.rearrange("o (h x) -> o h x", h=H),
           nrow_k[0:1, :].rearrange("o (h t) -> o h t", h=H), axis=AX.X)
    dve.op("tensor_tensor",
           negc[0:1, 0:H * NTB].rearrange("o (h t) -> o h t", h=H),
           nrow_q[0:1, :].rearrange("o (h t) -> o h t", h=H),
           nkm.unsqueeze(2).broadcast_to([1, H, NTB]), ALU.mult)
    act.after(dve)
    act.op("activation", negc[0:1, 0:H * NTB], negc[0:1, 0:H * NTB], AF.Sqrt)
    dve.after(act)
    dve.op("tensor_scalar_mul", negc[0:1, 0:H * NTB], negc[0:1, 0:H * NTB],
           -scale)

    vall = p.sbuf("at_vall", [128, NKT * QD], BF16)
    kbuf = p.sbuf("at_kb", [64, 2 * S], BF16)
    qbuf = p.sbuf("at_qb", [64, 2 * TB], BF16)
    vh = p.sbuf("at_vh", [128, HD + 1], BF16)
    ptb = p.sbuf("at_pt", [128, TB], BF16)
    sd = p.sbuf("at_sd", [128, TB], F32)
    drow = p.sbuf("at_dr", [1, TB], F32)
    dbc = p.sbuf("at_dbc", [64, TB], F32)
    ccol = p.sbuf("at_ccol", [128, 1], F32)
    ocat = p.sbuf("at_ocat", [128, NC * TB], BF16)
    dist = p.sbuf("at_dist", [128, TB], F32) if layer == 0 else None
    cmsk = p.sbuf("at_cmsk", [128, TB], F32) if layer == 1 else None

    ps_s = bufs["ps"]
    ps_o = p.psum("at_ps_o", [HD + 1, TB])
    ps_bc = p.psum("at_ps_bc", [128, TB])
    ps_b = ps_bc[0:64, :]
    ps_c = ps_bc[:, 0:1]

    gp.op("memset", vh[:, HD:HD + 1], 1.0)

    sp.wait_dma("pst", st3)
    pe.wait_dma("pst", st3)

    if layer == 0:
        rows = p.sbuf("at_rows", [1, 8 * TB], F32)
        absb = p.sbuf("at_absb", [128, 2 * TB], F32)
        aout_t = p.sbuf("at_aot", [128, TB], BF16)
        sqc = p.sbuf("at_sqc", [128, TB], F32)
        ps_sx = bufs["psn"]
        ps_sq = p.psum("at_ps_sq", [1, TB])
        ps_a = p.psum("at_ps_a", [128, TB])
        ps_b2 = p.psum("at_ps_b2", [128, TB])
    else:
        woT = p.sbuf("at_wo", [128, NC * QD], BF16)
        wof = bufs["wf"]
        wo = p.tensors["w_o"]
        for c in range(NC):
            sp.after(gp)
            sp.dma(wof[:, 0:QD], wo[c * 128:(c + 1) * 128, :], "ldw")
            gp.wait_dma("ldw").after(pe)
            gp.op("tensor_copy", woT[:, c * QD:(c + 1) * QD], wof[:, 0:QD])
        yps = bufs["ps2"]
        yev = p.sbuf("at_yev", [128, TB], BF16)

    for qb in range(NTB):
        q0 = qb * TB
        if qb == 0:
            sp.after(gp)
            sp.dma(vall[:].rearrange("p (kt d) -> p kt d", d=QD),
                   v_tm.rearrange("(kt p) d -> p kt d", p=128), "vld")
            gp.wait_dma("vld")
        for h in range(H):
            kslot = (qb * H + h) % 2
            sp.after(pe)
            sp.dma(qbuf[:, kslot * TB:kslot * TB + TB],
                   q_fm[h * HD:h * HD + HD, q0:q0 + TB], "qld")
            nkt_here = NKT if layer == 0 else min((q0 + TB + 127) // 128, NKT)
            sp.dma(kbuf[:, kslot * S:kslot * S + nkt_here * 128],
                   k_fm[h * HD:h * HD + HD, 0:nkt_here * 128], "kld")
            idx = h * NTB + qb
            pe.after(dve)
            pe.op("matmul", ps_c, onr[:], negc[0:1, idx:idx + 1],
                  start=True, stop=True)
            dve.after(pe)
            dve.op("tensor_copy", ccol[:], ps_c)
            pe.wait_dma("qld").wait_dma("kld")
            first_kt = True
            for kt in range(nkt_here):
                k0 = kt * 128
                pe.after(act if not first_kt else dve)
                pe.op("matmul", ps_s[:],
                      kbuf[:, kslot * S + k0:kslot * S + k0 + 128][0:HD, :],
                      qbuf[:, kslot * TB:kslot * TB + TB][0:HD, :],
                      start=True, stop=True)
                if layer == 0:
                    gp.after(dve)
                    gp.op("iota", dist[:], [[-1, TB]], base=k0 - q0,
                          channel_multiplier=1,
                          allow_small_or_imprecise_dtypes=True)
                    dve.after(pe).after(gp).after(act)
                    dve.op("tensor_single_scalar", dist[:].bitcast(I32),
                           dist[:].bitcast(I32), 0x7FFFFFFF, ALU.bitwise_and)
                    dve.op("tensor_mul", sd[:], ps_s[:], dist[:])
                    act.after(dve)
                    act.op("activation", ptb[:], sd[:], AF.Exp, bias=ccol[:],
                           scale=scale0 / S)
                elif k0 + 127 >= q0:
                    gp.after(dve)
                    gp.op("iota", cmsk[:], [[1, TB]], base=q0 - k0,
                          channel_multiplier=-1,
                          allow_small_or_imprecise_dtypes=True)
                    dve.after(pe).after(act).after(gp)
                    dve.op("tensor_scalar", cmsk[:], cmsk[:], 0.0, 1.0e28,
                           ALU.min, ALU.mult)
                    dve.op("tensor_add", sd[:], ps_s[:], cmsk[:])
                    act.after(dve)
                    act.op("activation", ptb[:], sd[:], AF.Exp, bias=ccol[:],
                           scale=scale1)
                else:
                    act.after(pe)
                    if first_kt:
                        act.after(dve)
                    act.op("activation", ptb[:], ps_s[:], AF.Exp,
                           bias=ccol[:], scale=scale1)
                gp.after(pe)
                gp.op("tensor_copy", vh[:, 0:HD],
                      vall[:, kt * QD + h * HD:kt * QD + h * HD + HD])
                pe.after(act).after(gp)
                pe.op("matmul", ps_o[:], vh[:], ptb[:],
                      start=first_kt, stop=(kt == nkt_here - 1))
                first_kt = False
            dve.after(pe)
            dve.op("reciprocal", drow[:], ps_o[HD:HD + 1, :])
            pe.after(dve)
            pe.op("matmul", ps_b, onr[0:1, 0:64], drow[:], start=True, stop=True)
            dve.after(pe)
            dve.op("tensor_copy", dbc[:], ps_b)
            oc, half = h // 2, h % 2
            dve.op("tensor_mul",
                   ocat[:, oc * TB:(oc + 1) * TB][half * 64:half * 64 + 64, :],
                   ps_o[0:HD, :], dbc[:])

        if layer == 0:
            act.after(dve)
            for c in range(NC):
                if c > 0:
                    act.after(pe)
                act.op("activation", sqc[:], ocat[:, c * TB:(c + 1) * TB],
                       AF.Square)
                pe.after(act).after(dve)
                pe.op("matmul", ps_sx[:], oncb[:], ocat[:, c * TB:(c + 1) * TB],
                      start=(c == 0), stop=(c == NC - 1))
                pe.op("matmul", ps_sq[:], onc[:], sqc[:],
                      start=(c == 0), stop=(c == NC - 1))
            m_r = rows[:, 0:TB]
            v_r = rows[:, TB:2 * TB]
            a_r = rows[:, 2 * TB:3 * TB]
            b_r = rows[:, 3 * TB:4 * TB]
            t_r = rows[:, 4 * TB:5 * TB]
            dve.after(pe)
            dve.op("tensor_scalar_mul", m_r, ps_sx[:], 1.0 / D)
            dve.op("tensor_scalar_mul", v_r, ps_sq[:], 1.0 / D)
            dve.op("tensor_mul", t_r, m_r, m_r)
            dve.op("tensor_sub", v_r, v_r, t_r)
            dve.op("tensor_scalar_add", v_r, v_r, cfg["EPS"])
            dve.op("reciprocal", t_r, v_r)
            act.after(dve).op("activation", a_r, t_r, AF.Sqrt)
            dve.after(act)
            dve.op("tensor_mul", t_r, m_r, a_r)
            dve.op("tensor_scalar_mul", b_r, t_r, -1.0)
            pe.after(dve)
            pe.op("matmul", ps_a[:], onr[:], a_r, start=True, stop=True)
            pe.op("matmul", ps_b2[:], onr[:], b_r, start=True, stop=True)
            dve.after(pe)
            dve.op("tensor_copy", absb[:, 0:TB], ps_a[:])
            dve.op("tensor_copy", absb[:, TB:2 * TB], ps_b2[:])
            for c in range(NC):
                dve.op("tensor_mul", sqc[:], ocat[:, c * TB:(c + 1) * TB],
                       absb[:, 0:TB])
                dve.op("tensor_add", sqc[:], sqc[:], absb[:, TB:2 * TB])
                dve.wait_dma("ast")
                dve.op("tensor_scalar", aout_t[:], sqc[:], gb[:, c:c + 1],
                       gb[:, NC + c:NC + c + 1], ALU.mult, ALU.add)
                sp.after(dve)
                sp.dma(a_out[c * 128:(c + 1) * 128, q0:q0 + TB], aout_t[:],
                       "ast")
        else:
            for oc in range(NC):
                pe.after(dve).after(act)
                for c in range(NC):
                    pe.op("matmul", yps[:],
                          woT[:, c * QD + oc * 128:c * QD + oc * 128 + 128],
                          ocat[:, c * TB:(c + 1) * TB],
                          start=(c == 0), stop=(c == NC - 1))
                act.after(pe).wait_dma("ast")
                act.op("activation", yev[:], yps[:], AF.Copy)
                sp.after(act)
                sp.dma(a_out[oc * 128:(oc + 1) * 128, q0:q0 + TB], yev[:],
                       "ast")

    nc.sync.wait_ge(p.sem("ast"), p.val("ast"))


def build_attn(cfg, layer: int):
    p = Prog()
    S, D, H, HD = cfg["S"], cfg["D"], cfg["H"], cfg["HD"]
    QD = H * HD
    declare_consts(p)
    p.din("xn_fm", [D, S], BF16)
    if layer == 0:
        p.din("w_q", [D, QD], F32)
        p.din("w_k", [D, QD], F32)
        p.din("w_v", [D, QD], F32)
        p.din("ln_gamma", [D, 1], F32)
        p.din("ln_beta", [D, 1], F32)
    else:
        LAT = cfg["LAT"]
        p.din("w_q", [D, QD], F32)
        p.din("w_q2", [D, QD], F32)
        p.din("w_kvd", [D, LAT], F32)
        p.din("w_ku", [LAT, QD], F32)
        p.din("w_ku2", [LAT, QD], F32)
        p.din("w_vu", [LAT, QD], F32)
        p.din("w_o", [QD, D], F32)
        p.din("c_cos2", [128, S], F32)
        p.din("c_sin2", [128, S], F32)
    p.dout("attn_fm", [D, S], BF16)
    emit_attn(p, cfg, layer)
    p.finish()
    return p



def emit_ffn(p: Prog, cfg, layer: int):
    nc = p.nc
    pe, dve, act, gp, sp = p.chains()
    S, D = cfg["S"], cfg["D"]
    FTB = cfg["FTB"]
    NC = D // 128
    NFT = S // FTB
    if layer == 0:
        E, F = cfg["E0"], cfg["F0"]
        ET = E
        afunc = AF.Silu
    else:
        E, F = cfg["E1"], cfg["HID1"]
        ET = E + 1
        afunc = AF.Gelu
    NCF = (F + 127) // 128
    NTOK = cfg["NTOK"]
    RT = NTOK // 128
    KSEL = NTOK // cfg["E0"]

    xn = p.tensors["xn_fm"]
    f_out = p.tensors["ffn_fm"]
    rout = p.tensors["probs" if layer == 0 else "combine"]

    onc = p.sbuf("sb_onc", [128, 1], F32)
    onr = p.sbuf("sb_onr", [1, 128], F32)
    idn = p.sbuf("sb_idn", [128, 128], F32)
    sp.dma(onc[:], p.tensors["c_ones_col"], "ld")
    sp.dma(onr[:], p.tensors["c_ones_row"], "ld")
    sp.dma(idn[:], p.tensors["c_iden"], "ld")

    gmT = p.sbuf("ff_gmT", [ET, S], F32)
    bdall = p.sbuf("ff_bd", [ET, D], F32)
    rtile = p.sbuf("ff_rt", [128, 2 * E], F32)
    thr_b = p.sbuf("ff_thrb", [128, E], F32)
    ev = p.sbuf("ff_ev", [128, FTB], BF16)

    ps = p.psum("ff_ps", [128, NC * FTB])
    ps0 = ps[:, 0:FTB]

    if layer == 0:
        sp.dma(bdall[:], p.tensors["b_down"], "ld")
    else:
        sp.dma(bdall[0:E, :], p.tensors["b2"], "ld")
        sp.dma(bdall[E:E + 1, :], p.tensors["bs2"], "ld")
    pe.wait_dma("ld")
    dve.wait_dma("ld")
    gp.wait_dma("ld")

    if layer == 0:
        pall = p.sbuf("ff_pall", [128, RT * E], F32)
        msk = p.sbuf("ff_msk", [128, RT * E], F32)
        cnt = p.sbuf("ff_cnt", [128, E], F32)
        rows = p.sbuf("ff_rows", [1, 8 * E], F32)
        sp.dma(pall[:].rearrange("p (r e) -> p r e", e=E),
               p.tensors["probs_all"].rearrange("(r p) e -> p r e", p=128),
               "pald")
        lo = rows[:, 0:E]
        hi = rows[:, E:2 * E]
        md = rows[:, 2 * E:3 * E]
        ge = rows[:, 3 * E:4 * E]
        t1 = rows[:, 4 * E:5 * E]
        cr = rows[:, 5 * E:6 * E]
        dve.wait_dma("pald")
        dve.op("memset", lo, 0.0)
        dve.op("memset", hi, 1.0)
        pv = pall[:].rearrange("p (r e) -> p r e", e=E)
        mv = msk[:].rearrange("p (r e) -> p r e", e=E)
        mrv = msk[:].rearrange("p (r e) -> p e r", e=E)
        for it in range(30):
            dve.op("tensor_add", md, lo, hi)
            dve.op("tensor_scalar_mul", md, md, 0.5)
            pe.after(dve)
            pe.op("matmul", ps0[:, 0:E], onr[:], md, start=True, stop=True)
            dve.after(pe)
            dve.op("tensor_copy", thr_b[:], ps0[:, 0:E])
            dve.op("tensor_tensor", mv, pv,
                   thr_b[:].unsqueeze(1).broadcast_to([128, RT, E]), ALU.is_ge)
            dve.op("reduce_sum", cnt[:], mrv, axis=AX.X)
            pe.after(dve)
            pe.op("matmul", ps0[0:1, 0:E], onc[:], cnt[:],
                  start=True, stop=True)
            dve.after(pe)
            dve.op("tensor_copy", cr, ps0[0:1, 0:E])
            dve.op("tensor_single_scalar", ge, cr, float(KSEL), ALU.is_ge)
            dve.op("tensor_sub", t1, md, lo)
            dve.op("tensor_mul", t1, ge, t1)
            dve.op("tensor_add", lo, lo, t1)
            dve.op("tensor_sub", t1, hi, md)
            dve.op("tensor_mul", t1, ge, t1)
            dve.op("tensor_add", hi, md, t1)
        pe.after(dve)
        pe.op("matmul", ps0[:, 0:E], onr[:], lo, start=True, stop=True)
        dve.after(pe)
        dve.op("tensor_copy", thr_b[:], ps0[:, 0:E])

    for tt in range(S // 128):
        sp.after(pe).after(dve)
        sp.dma(rtile[:, 0:E], rout[tt * 128:(tt + 1) * 128, :], "rld")
        dve.wait_dma("rld")
        gmt = rtile[:, E:2 * E]
        if layer == 0:
            dve.op("tensor_tensor", gmt, rtile[:, 0:E], thr_b[:], ALU.is_ge)
            dve.op("tensor_mul", gmt, gmt, rtile[:, 0:E])
        else:
            dve.op("tensor_copy", gmt, rtile[:, 0:E])
        pe.after(dve)
        pe.op("transpose", ps0[0:E, 0:128], gmt, idn[:])
        dve.after(pe)
        dve.op("tensor_copy", gmT[0:E, tt * 128:(tt + 1) * 128],
               ps0[0:E, 0:128])
    if layer == 1:
        sp.dma(gmT[E:E + 1, :], p.tensors["c_ones_srow"], "orow")
        dve.wait_dma("orow")
        dve.op("engine_nop")
    gm_dram = p.dtmp("gm_dram", [ET, S], F32)
    sp.after(dve)
    sp.dma(gm_dram, gmT[:], "gmst")

    for tb in range(NFT):
        t0 = tb * FTB
        for oc in range(NC):
            pe.after(dve)
            pe.op("matmul", ps0[:], bdall[:, oc * 128:(oc + 1) * 128],
                  gmT[:, t0:t0 + FTB], start=True, stop=True)
            dve.after(pe).wait_dma("fst")
            dve.op("tensor_copy", ev[:], ps0[:])
            sp.after(dve)
            sp.dma(f_out[oc * 128:(oc + 1) * 128, t0:t0 + FTB], ev[:], "fst")

    if layer == 0:
        groups = [dict(up=p.tensors["w_up"][e // 2],
                       bup=p.tensors["b_up_r"][e // 2],
                       dn=p.tensors["w_down"][e], idx=e, gm=True)
                  for e in range(E)]
    else:
        groups = [dict(up=p.tensors["w1"][e], bup=p.tensors["b1_r"][e],
                       dn=p.tensors["w2"][e], idx=e, gm=True)
                  for e in range(E)]
        groups.append(dict(up=p.tensors["ws1"], bup=p.tensors["bs1_r"][0],
                           dn=p.tensors["ws2"], idx=E, gm=False))

    NCI = D // 128
    upb = p.sbuf("ff_upb", [128, NCI * NCF * 128], BF16)
    dnb = p.sbuf("ff_dnb", [128, NCF * D], BF16)
    wf = p.sbuf("ff_wf", [128, max(D, min(NCF * 128, 2048))], F32)
    bupc = p.sbuf("ff_bupc", [128, NCF], F32)
    bupr = p.sbuf("ff_bupr", [NCF, 128], F32)
    xb = p.sbuf("ff_xb", [128, NCI * FTB], BF16)
    midb = p.sbuf("ff_midb", [128, NCF * FTB], BF16)
    gmbc = p.sbuf("ff_gmbc", [128, FTB], F32)
    grow = p.sbuf("ff_grow", [1, FTB], F32)

    for gi, g in enumerate(groups):
        facc_prev = p.val("facc")
        FH = min(F, 2048)
        for c in range(NCI):
            for fh0 in range(0, F, FH):
                fl = min(FH, F - fh0)
                sp.after(gp)
                sp.dma(wf[:, 0:fl], g["up"][c * 128:(c + 1) * 128,
                                            fh0:fh0 + fl], "wld")
                gp.wait_dma("wld").after(pe)
                fc0 = fh0 // 128
                for fc in range(fc0, (fh0 + fl + 127) // 128):
                    lo_ = max(fc * 128 - fh0, 0)
                    hiw = min((fc + 1) * 128, fh0 + fl) - fh0
                    gp.op("tensor_copy",
                          upb[:, (c * NCF + fc) * 128 + (lo_ + fh0 - fc * 128):
                              (c * NCF + fc) * 128 + (hiw + fh0 - fc * 128)],
                          wf[:, lo_:hiw])
        sp.after(pe).after(dve).after(gp)
        sp.dma(bupr[:], g["bup"], "wld")
        gp.wait_dma("wld")
        gp.op("engine_nop")
        pe.after(gp).after(dve).after(act)
        pe.op("transpose", ps0[:, 0:NCF], bupr[:], idn[0:NCF, 0:NCF])
        dve.after(pe)
        dve.op("tensor_copy", bupc[:], ps0[:, 0:NCF])
        for fc in range(NCF):
            fw = min(128, F - fc * 128)
            sp.after(gp)
            sp.dma(wf[0:fw, 0:D], g["dn"][fc * 128:fc * 128 + fw, :], "wld")
            gp.wait_dma("wld").after(pe)
            gp.op("tensor_copy", dnb[0:fw, fc * D:fc * D + D], wf[0:fw, 0:D])

        for tb in range(NFT):
            t0 = tb * FTB
            sp.after(pe)
            for c in range(NCI):
                sp.dma(xb[:, c * FTB:(c + 1) * FTB],
                       xn[c * 128:(c + 1) * 128, t0:t0 + FTB], "xld")
            pe.wait_dma("xld").after(dve).after(gp)
            for fc in range(NCF):
                fw = min(128, F - fc * 128)
                pe.after(act)
                for c in range(NCI):
                    pe.op("matmul", ps0[0:fw, :],
                          upb[:, (c * NCF + fc) * 128:(c * NCF + fc) * 128 + fw],
                          xb[:, c * FTB:(c + 1) * FTB],
                          start=(c == 0), stop=(c == NCI - 1))
                act.after(pe)
                act.op("activation", midb[0:fw, fc * FTB:(fc + 1) * FTB],
                       ps0[0:fw, :], afunc, bias=bupc[0:fw, fc:fc + 1])
            if g["gm"]:
                sp.after(pe).wait_dma("gmst")
                sp.dma(grow[:], gm_dram[g["idx"], t0:t0 + FTB].unsqueeze(0),
                       "gld")
                pe.wait_dma("gld").after(act).after(gp)
                pe.op("matmul", ps0[:], onr[:], grow[:], start=True, stop=True)
                dve.after(pe)
                dve.op("tensor_copy", gmbc[:], ps0[:])
                gp.after(dve)
                for fc in range(NCF):
                    fw = min(128, F - fc * 128)
                    gp.op("tensor_mul", midb[0:fw, fc * FTB:(fc + 1) * FTB],
                          midb[0:fw, fc * FTB:(fc + 1) * FTB], gmbc[0:fw, :])
                pe.after(gp)
            else:
                pe.after(act)
            pe.after(dve)
            for oc in range(NC):
                for fc in range(NCF):
                    fw = min(128, F - fc * 128)
                    pe.op("matmul", ps[:, oc * FTB:(oc + 1) * FTB],
                          dnb[0:fw, fc * D + oc * 128:fc * D + oc * 128 + 128],
                          midb[0:fw, fc * FTB:(fc + 1) * FTB],
                          start=(fc == 0), stop=(fc == NCF - 1))
            for oc in range(NC):
                dve.after(pe).wait_dma("facc").wait_dma("fst")
                dve.op("tensor_copy", ev[:], ps[:, oc * FTB:(oc + 1) * FTB])
                gp.after(dve).wait_dma("facc", facc_prev)
                if gi == 0:
                    gp.wait_dma("fst")
                gp.dma(f_out[oc * 128:(oc + 1) * 128, t0:t0 + FTB], ev[:],
                       "facc", accum_op=ALU.add)

    nc.sync.wait_ge(p.sem("facc"), p.val("facc"))
    nc.sync.wait_ge(p.sem("fst"), p.val("fst"))


def build_ffn(cfg, layer: int):
    p = Prog()
    S, D = cfg["S"], cfg["D"]
    declare_consts(p)
    p.din("c_iden", [128, 128], F32)
    p.din("xn_fm", [D, S], BF16)
    if layer == 0:
        E, F, NG = cfg["E0"], cfg["F0"], cfg["NG"]
        NCF = (F + 127) // 128
        p.din("probs", [S, E], F32)
        p.din("probs_all", [cfg["NTOK"], E], F32)
        p.din("w_up", [NG, D, F], F32)
        p.din("b_up_r", [NG, NCF, 128], F32)
        p.din("w_down", [E, F, D], F32)
        p.din("b_down", [E, D], F32)
    else:
        E, F = cfg["E1"], cfg["HID1"]
        NCF = (F + 127) // 128
        p.din("combine", [S, E], F32)
        p.din("w1", [E, D, F], F32)
        p.din("b1_r", [E, NCF, 128], F32)
        p.din("w2", [E, F, D], F32)
        p.din("b2", [E, D], F32)
        p.din("ws1", [D, F], F32)
        p.din("bs1_r", [1, NCF, 128], F32)
        p.din("ws2", [F, D], F32)
        p.din("bs2", [1, D], F32)
        p.din("c_ones_srow", [1, S], F32)
    p.dout("ffn_fm", [D, S], BF16)
    emit_ffn(p, cfg, layer)
    p.finish()
    return p



def emit_resid(p: Prog, cfg):
    nc = p.nc
    pe, dve, act, gp, sp = p.chains()
    S, D, TB = cfg["S"], cfg["D"], cfg["TB"]
    NC = D // 128
    x = p.tensors["x_fm"]
    a = p.tensors["attn_fm"]
    f = p.tensors["ffn_fm"]
    xo = p.tensors["xo_fm"]
    xb = p.sbuf("rs_x", [128, TB], F32)
    ab = p.sbuf("rs_a", [128, TB], BF16)
    fb = p.sbuf("rs_f", [128, TB], BF16)
    tb_ = p.sbuf("rs_t", [128, TB], F32)
    ob = p.sbuf("rs_o", [128, TB], F32)
    for c in range(NC):
        for t in range(S // TB):
            t0 = t * TB
            sp.after(gp)
            sp.dma(xb[:], x[c * 128:(c + 1) * 128, t0:t0 + TB], "xld")
            sp.dma(ab[:], a[c * 128:(c + 1) * 128, t0:t0 + TB], "xld")
            sp.dma(fb[:], f[c * 128:(c + 1) * 128, t0:t0 + TB], "xld")
            gp.wait_dma("xld").wait_dma("ost")
            gp.op("tensor_add", tb_[:], ab[:], fb[:])
            gp.op("tensor_add", ob[:], tb_[:], xb[:])
            sp.after(gp)
            sp.dma(xo[c * 128:(c + 1) * 128, t0:t0 + TB], ob[:], "ost")
    nc.sync.wait_ge(p.sem("ost"), p.val("ost"))


def build_resid(cfg):
    p = Prog()
    S, D = cfg["S"], cfg["D"]
    p.din("x_fm", [D, S], F32)
    p.din("attn_fm", [D, S], BF16)
    p.din("ffn_fm", [D, S], BF16)
    p.dout("xo_fm", [D, S], F32)
    emit_resid(p, cfg)
    p.finish()
    return p



def build_ln_router(cfg, layer: int):
    p = Prog()
    S, D = cfg["S"], cfg["D"]
    E = cfg["E0"] if layer == 0 else cfg["E1"]
    declare_consts(p)
    p.din("x_fm", [D, S], F32)
    p.din("w_r", [D, E], F32)
    p.din("b_r", [1, E], F32)
    p.dout("xn_fm", [D, S], BF16)
    p.dout("probs" if layer == 0 else "combine", [S, E], F32)
    emit_ln_router(p, cfg, layer)
    p.finish()
    return p



def _fm(x2d):
    return np.ascontiguousarray(x2d.T)


def _lhsT(w_out_in):
    return np.ascontiguousarray(w_out_in.T)


def _swap_halves(wT, HD):
    Din, QD = wT.shape
    w3 = wT.reshape(Din, QD // HD, HD)
    return np.ascontiguousarray(
        np.concatenate([w3[..., HD // 2:], w3[..., :HD // 2]], axis=-1)
        .reshape(Din, QD))


def _pad_bias(b2d, ncf):
    out = np.zeros((b2d.shape[0], ncf * 128), np.float32)
    out[:, :b2d.shape[1]] = b2d
    return out.reshape(b2d.shape[0], ncf, 128)


def _rope_tables(S, HD, theta):
    inv = 1.0 / theta ** (np.arange(0, HD, 2, dtype=np.float32) / HD)
    t = np.arange(S, dtype=np.float32)
    j = np.arange(128) % HD
    f = inv[(j % (HD // 2))]
    ang = t[None, :] * f[:, None]
    return (np.ascontiguousarray(np.cos(ang).astype(np.float32)),
            np.ascontiguousarray(np.sin(ang).astype(np.float32)))



def _forward(inputs, cfg, run):
    R, S, D, HD = cfg["R"], cfg["S"], cfg["D"], cfg["HD"]
    NCF0 = (cfg["F0"] + 127) // 128
    NCF1 = (cfg["HID1"] + 127) // 128
    consts = const_arrays()
    iden = np.eye(128, dtype=np.float32)
    x = np.asarray(inputs["L0_attn_x"], dtype=np.float32)
    x_fm = [_fm(x[i]) for i in range(R)]

    p = build_ln_router(cfg, 0)
    im = [dict(consts, x_fm=x_fm[i],
               w_r=np.ascontiguousarray(inputs["L0_ffn_Wr"]),
               b_r=np.asarray(inputs["L0_ffn_br"]).reshape(1, -1))
          for i in range(R)]
    r0 = run(p, im)
    xn0 = [r["xn_fm"] for r in r0]
    probs = [np.asarray(r["probs"], np.float32) for r in r0]
    probs_all = np.ascontiguousarray(np.concatenate(probs, axis=0))

    p = build_attn(cfg, 0)
    im = [dict(consts, xn_fm=xn0[i],
               w_q=_lhsT(inputs["L0_attn_Wq"]),
               w_k=_lhsT(inputs["L0_attn_Wk"]),
               w_v=_lhsT(inputs["L0_attn_Wv"]),
               ln_gamma=np.ascontiguousarray(
                   np.asarray(inputs["L0_attn_ln_gamma"]).reshape(D, 1)),
               ln_beta=np.ascontiguousarray(
                   np.asarray(inputs["L0_attn_ln_beta"]).reshape(D, 1)))
          for i in range(R)]
    a0 = [r["attn_fm"] for r in run(p, im)]

    p = build_ffn(cfg, 0)
    im = [dict(consts, c_iden=iden, xn_fm=xn0[i], probs=probs[i],
               probs_all=probs_all,
               w_up=np.ascontiguousarray(inputs["L0_ffn_W_up"]),
               b_up_r=_pad_bias(np.asarray(inputs["L0_ffn_b_up"]), NCF0),
               w_down=np.ascontiguousarray(inputs["L0_ffn_W_down"]),
               b_down=np.ascontiguousarray(inputs["L0_ffn_b_down"]))
          for i in range(R)]
    f0 = [r["ffn_fm"] for r in run(p, im)]

    p = build_resid(cfg)
    im = [dict(x_fm=x_fm[i], attn_fm=a0[i], ffn_fm=f0[i]) for i in range(R)]
    x1 = [r["xo_fm"] for r in run(p, im)]

    p = build_ln_router(cfg, 1)
    im = [dict(consts, x_fm=x1[i],
               w_r=np.ascontiguousarray(inputs["L1_ffn_gate_w"]),
               b_r=np.asarray(inputs["L1_ffn_gate_b"]).reshape(1, -1))
          for i in range(R)]
    r1 = run(p, im)
    xn1 = [r["xn_fm"] for r in r1]
    comb = [np.asarray(r["combine"], np.float32) for r in r1]

    cos2, sin2 = _rope_tables(S, HD, cfg["ROPE_THETA"])
    wqT = _lhsT(inputs["L1_attn_wq"])
    wkuT = _lhsT(inputs["L1_attn_w_k_up"])
    p = build_attn(cfg, 1)
    im = [dict(consts, xn_fm=xn1[i],
               w_q=wqT, w_q2=_swap_halves(wqT, HD),
               w_kvd=_lhsT(inputs["L1_attn_w_kv_down"]),
               w_ku=wkuT, w_ku2=_swap_halves(wkuT, HD),
               w_vu=_lhsT(inputs["L1_attn_w_v_up"]),
               w_o=_lhsT(inputs["L1_attn_wo"]),
               c_cos2=cos2, c_sin2=sin2)
          for i in range(R)]
    a1 = [r["attn_fm"] for r in run(p, im)]

    p = build_ffn(cfg, 1)
    im = [dict(consts, c_iden=iden, xn_fm=xn1[i], combine=comb[i],
               w1=np.ascontiguousarray(inputs["L1_ffn_w1"]),
               b1_r=_pad_bias(np.asarray(inputs["L1_ffn_b1"]), NCF1),
               w2=np.ascontiguousarray(inputs["L1_ffn_w2"]),
               b2=np.ascontiguousarray(inputs["L1_ffn_b2"]),
               ws1=np.ascontiguousarray(inputs["L1_ffn_ws1"]),
               bs1_r=_pad_bias(np.asarray(inputs["L1_ffn_bs1"]).reshape(1, -1),
                               NCF1),
               ws2=np.ascontiguousarray(inputs["L1_ffn_ws2"]),
               bs2=np.ascontiguousarray(
                   np.asarray(inputs["L1_ffn_bs2"]).reshape(1, D)),
               c_ones_srow=np.ones((1, S), np.float32))
          for i in range(R)]
    f1 = [r["ffn_fm"] for r in run(p, im)]

    p = build_resid(cfg)
    im = [dict(x_fm=x1[i], attn_fm=a1[i], ffn_fm=f1[i]) for i in range(R)]
    xo = [np.asarray(r["xo_fm"], np.float32) for r in run(p, im)]

    return np.stack([o.T for o in xo], axis=0)


def _hw_run(prog, in_maps):
    res = run_bass_kernel_spmd(prog.nc, in_maps,
                               core_ids=list(range(len(in_maps))))
    return res.results


def kernel(**inputs):
    cfg = default_cfg()
    return _forward(inputs, cfg, _hw_run)
